# version 01
# baseline (speedup 1.0000x reference)
import math
import numpy as np
import concourse.bass as bass
import concourse.mybir as mybir
from concourse.bass_utils import run_bass_kernel_spmd

F32 = mybir.dt.float32
BF16 = mybir.dt.bfloat16
AF = mybir.ActivationFunctionType
ALU = mybir.AluOpType
AX = mybir.AxisListType

NCORES = 8
T = 4096
D = 1024
CH = 512
NCHUNK = T // CH
NT = CH // 128

GH = 4
HK = 256
HV = 512
DV = 2048
LR = 16
EPS = 1e-5
LN16 = math.log(16.0)

AH = 16
DH = 64
INNER = 1024
WIN = 256
DI = 2730
DIP = 2816
NJ = DIP // 128

NEG = -3.0e38


class PSem:
    def __init__(self, name):
        self.name = name
        self.val = 0
        self.h = None


class Prog:
    ENGINES = ("pe", "dve", "act", "pool", "sp")

    def __init__(self):
        self.ops = {e: [] for e in self.ENGINES}
        self.sems = {}

    def sem(self, name):
        if name not in self.sems:
            self.sems[name] = PSem(name)
        return self.sems[name]

    def emit(self, eng, fn, waits=(), inc=None, inc_n=1, tag=""):
        w = []
        for it in waits:
            if it is None:
                continue
            s, v = it
            if v <= 0:
                continue
            w.append((s if isinstance(s, PSem) else self.sem(s), v))
        i = None
        if inc is not None:
            i = inc if isinstance(inc, PSem) else self.sem(inc)
            i.val += inc_n
        self.ops[eng].append([fn, w, i, inc_n, tag])

    def mark(self, sem):
        s = sem if isinstance(sem, PSem) else self.sem(sem)
        return (s, s.val)

    def replay(self, eng, handle):
        waited = {}
        for fn, waits, inc, inc_n, tag in self.ops[eng]:
            for s, v in waits:
                if waited.get(s.name, 0) >= v:
                    continue
                handle.wait_ge(s.h, v)
                waited[s.name] = v
            try:
                inst = fn(handle)
            except Exception as e:
                raise RuntimeError(f"emit failed at {eng}:{tag}: {e}") from e
            if inc is not None:
                assert inst is not None, f"inc without inst: {eng}:{tag}"
                inst.then_inc(inc.h, inc_n)


class Bank:

    def __init__(self, prog, ap, name):
        self.prog = prog
        self.ap = ap
        self.name = name
        self._wmark = None
        self._rmarks = []
        self._round_open = False

    def open_round(self):
        assert not self._round_open, f"bank {self.name}: round already open"
        self._round_open = True
        self._first = True
        r, self._rmarks = self._rmarks, []
        return r

    def start_flag(self):
        f = self._first
        self._first = False
        return f

    def close_round(self, mark):
        assert self._round_open
        self._round_open = False
        self._wmark = mark

    def r(self):
        assert not self._round_open, f"bank {self.name}: read during open round"
        return [self._wmark] if self._wmark else []

    def read(self, mark):
        self._rmarks.append(mark)


def host_prep(inputs):
    f32 = np.float32
    x = np.asarray(inputs["L0_attn_x"], dtype=f32)
    per_core_x = [np.ascontiguousarray(x[b].T) for b in range(NCORES)]

    W = {}
    W["wq"] = np.ascontiguousarray(inputs["L0_attn_Wq"], f32)
    W["wk"] = np.ascontiguousarray(inputs["L0_attn_Wk"], f32)
    W["wv"] = np.ascontiguousarray(inputs["L0_attn_Wv"], f32)
    W["wg"] = np.ascontiguousarray(inputs["L0_attn_Wg"], f32)
    W["gkw1"] = np.ascontiguousarray(inputs["L0_attn_gk_w1"], f32)
    W["gkw2"] = np.ascontiguousarray(inputs["L0_attn_gk_w2"], f32)
    W["gkb"] = np.asarray(inputs["L0_attn_gk_b"], f32).reshape(1, D)
    W["wo_gla"] = np.ascontiguousarray(inputs["L0_attn_Wo"], f32)
    rmsw = np.asarray(inputs["L0_attn_rms_w"], f32)
    W["rmsw_cols"] = np.ascontiguousarray(rmsw.reshape(4, 128).T)

    W["wqkv"] = np.ascontiguousarray(inputs["L0_ffn_Wqkv"], f32)
    W["qs_row"] = np.asarray(inputs["L0_ffn_q_scale"], f32).reshape(6, 1, DH)
    W["ks_row"] = np.asarray(inputs["L0_ffn_k_scale"], f32).reshape(6, 1, DH)
    W["wo_ffn"] = np.ascontiguousarray(inputs["L0_ffn_Wo"], f32)
    W["ln_g"] = np.ascontiguousarray(
        np.asarray(inputs["L0_ffn_ln_g"], f32).reshape(6, 8, 128).transpose(0, 2, 1))
    W["ln_b"] = np.ascontiguousarray(
        np.asarray(inputs["L0_ffn_ln_b"], f32).reshape(6, 8, 128).transpose(0, 2, 1))
    w1 = np.asarray(inputs["L0_ffn_W1"], f32)
    w1ag = np.zeros((6, D, 2 * DIP), f32)
    for j in range(NJ):
        lo, hi = j * 128, (j + 1) * 128
        lo_s = min(j * 128, DI)
        a_blk = np.zeros((6, D, 128), f32)
        g_blk = np.zeros((6, D, 128), f32)
        n = max(0, min(DI - j * 128, 128))
        if n > 0:
            a_blk[:, :, :n] = w1[:, :, j * 128: j * 128 + n]
            g_blk[:, :, :n] = w1[:, :, DI + j * 128: DI + j * 128 + n]
        w1ag[:, :, 2 * j * 128:(2 * j + 1) * 128] = a_blk
        w1ag[:, :, (2 * j + 1) * 128:(2 * j + 2) * 128] = g_blk
    W["w1ag"] = w1ag
    b1 = np.asarray(inputs["L0_ffn_b1"], f32)
    b1a = np.zeros((6, DIP), f32)
    b1g = np.zeros((6, DIP), f32)
    b1a[:, :DI] = b1[:, :DI]
    b1g[:, :DI] = b1[:, DI:]
    W["b1a_cols"] = np.ascontiguousarray(b1a.reshape(6, NJ, 128).transpose(0, 2, 1))
    W["b1g_cols"] = np.ascontiguousarray(b1g.reshape(6, NJ, 128).transpose(0, 2, 1))
    w2 = np.asarray(inputs["L0_ffn_W2"], f32)
    w2p = np.zeros((6, DIP, D), f32)
    w2p[:, :DI, :] = w2
    W["w2p"] = w2p
    W["b2"] = np.ascontiguousarray(
        np.asarray(inputs["L0_ffn_b2"], f32).reshape(6, 8, 128).transpose(0, 2, 1))

    C = {}
    C["ident"] = np.eye(128, dtype=f32)
    si = np.arange(128)[:, None]
    ti = np.arange(128)[None, :]
    C["lt16"] = ((si <= ti).astype(f32)) / 16.0
    C["tri01"] = (si <= ti).astype(f32)
    C["mask_first"] = np.where(si > ti, NEG, 0.0).astype(f32)
    C["mask_last"] = np.where(si < ti, NEG, 0.0).astype(f32)
    C["mask_neg"] = np.full((128, 128), NEG, f32)
    C["fconsts"] = np.array([[EPS, 1.0]], f32)
    C["fconsts_col"] = np.tile(np.array([[EPS, 1.0]], f32), (128, 1))
    C["ones_col"] = np.ones((128, 1), f32)
    C["ones_row"] = np.ones((1, 128), f32)
    half = np.arange(0, DH, 2, dtype=np.float64)
    inv_freq = 1.0 / 10000.0 ** (half / DH)
    tt = np.arange(T, dtype=np.float64)
    freqs = tt[:, None] * inv_freq[None, :]
    freqs = np.concatenate([freqs, freqs], -1)
    base_scale = (half + 0.4 * DH) / (1.4 * DH)
    power = (tt - T // 2) / (WIN // 2)
    scale = base_scale[None, :] ** power[:, None]
    scale = np.concatenate([scale, scale], -1)
    cs = np.cos(freqs) * scale
    ss = np.sin(freqs) * scale
    ci = np.cos(freqs) / scale
    sx = np.sin(freqs) / scale
    ssx = ss.copy()
    ssx[:, 0:32] = -ss[:, 0:32]
    ssx[:, 64:96] = -ss[:, 64:96] if ss.shape[1] > 64 else 0
    six = sx.copy()
    six[:, 0:32] = -sx[:, 0:32]
    C["rot_cs"] = cs.astype(f32)
    C["rot_ssx"] = ssx.astype(f32)
    C["rot_ci"] = ci.astype(f32)
    C["rot_six"] = six.astype(f32)
    return per_core_x, W, C


class KB:
    def __init__(self, n_layers=6):
        self.n_layers = n_layers
        self.nc = bass.Bass("TRN2", target_bir_lowering=False,
                            detect_race_conditions=False)
        self.p = Prog()
        self._io()
        self._sbuf()
        self._psum()
        self.ych_free = [None, None]
        self.ysh_free = [None, None]
        self.ws_free = [None, None]
        self.m_const = None

    def _io(self):
        nc = self.nc

        def d(n, s, dt=F32, k="ExternalInput"):
            return nc.dram_tensor(n, s, dt, kind=k).ap()

        self.x0 = d("x0", [D, T])
        self.debug = False
        self.dbg = [d(f"dbg{i}", [128, 4096], k="ExternalOutput") for i in range(4)]
        self.outp = d("outp", [D, T], k="ExternalOutput")
        self.xbuf = d("xbuf", [D, T], k="Internal")
        self.ybuf = [d("yA", [D, T], k="Internal"), d("yB", [D, T], k="Internal")]
        self.wq_f = d("wq", [D, D]); self.wk_f = d("wk", [D, D])
        self.wv_f = d("wv", [D, DV]); self.wg_f = d("wg", [D, DV])
        self.gkw1_f = d("gkw1", [D, LR]); self.gkw2_f = d("gkw2", [LR, D])
        self.gkb_f = d("gkb", [1, D]); self.wo_gla_f = d("wo_gla", [DV, D])
        self.rmsw_f = d("rmsw_cols", [128, 4])
        self.wqkv_f = d("wqkv", [6, D, 3 * INNER])
        self.qs_f = d("qs", [6, 1, DH]); self.ks_f = d("ks", [6, 1, DH])
        self.wo_ffn_f = d("wo_ffn", [6, INNER, D])
        self.ln_g_f = d("ln_g", [6, 128, 8]); self.ln_b_f = d("ln_b", [6, 128, 8])
        self.w1ag_f = d("w1ag", [6, D, 2 * DIP])
        self.b1a_f = d("b1a_cols", [6, 128, NJ]); self.b1g_f = d("b1g_cols", [6, 128, NJ])
        self.w2p_f = d("w2p", [6, DIP, D]); self.b2_f = d("b2", [6, 128, 8])
        self.ident_f = d("ident", [128, 128])
        self.lt16_f = d("lt16", [128, 128])
        self.tri01_f = d("tri01", [128, 128])
        self.mask_first_f = d("mask_first", [128, 128])
        self.mask_last_f = d("mask_last", [128, 128])
        self.mask_neg_f = d("mask_neg", [128, 128])
        self.ones_col_f = d("ones_col", [128, 1])
        self.fconsts_f = d("fconsts", [1, 2])
        self.fccol_f = d("fconsts_col", [128, 2])
        self.ones_row_f = d("ones_row", [1, 128])
        self.rot_cs_f = d("rot_cs", [T, DH]); self.rot_ssx_f = d("rot_ssx", [T, DH])
        self.rot_ci_f = d("rot_ci", [T, DH]); self.rot_six_f = d("rot_six", [T, DH])

    def _sbuf(self):
        nc = self.nc
        S = lambda n, s, dt: nc.alloc_sbuf_tensor(n, s, dt)
        self.ych = [S(f"ych{i}", [128, 8 * CH], F32) for i in range(2)]
        self.ysh = [S(f"ysh{i}", [128, 4 * CH], BF16) for i in range(2)]
        self.zbf = S("zbf", [128, 8 * CH], BF16)
        self.gobf = S("gobf", [128, 8 * CH], BF16)
        self.ws = [S(f"ws{i}", [128, 8 * CH], BF16) for i in range(2)]
        self.ztmp = [S(f"ztmp{i}", [128, CH], F32) for i in range(2)]
        self.rrep_sb = S("rrep_sb", [128, CH], F32)
        self.osq = S("osq", [128, 4 * CH], BF16)
        self.ident = S("identb", [128, 128], BF16)
        self.lt16 = S("lt16b", [128, 128], BF16)
        self.tri01 = S("tri01b", [128, 128], BF16)
        self.mask_first = S("mask_firstb", [128, 128], BF16)
        self.mask_last = S("mask_lastb", [128, 128], BF16)
        self.mask_neg = S("mask_negb", [128, 128], BF16)
        self.ones_col = S("ones_colb", [128, 1], BF16)
        self.ones_row = S("ones_rowb", [1, 128], BF16)
        self.gkb_row = S("gkb_row", [1, D], BF16)
        self.gkb_rep = S("gkb_rep", [128, D], F32)
        self.rmsw_sb = S("rmsw_sb", [128, 4], F32)
        self.fc = S("fc", [1, 2], F32)
        self.acsp = S("acsp", [1, 1], F32)
        self.fccol = S("fccol", [128, 2], F32)
        self.lng_sb = S("lng_sb", [128, 8], F32)
        self.lnb_sb = S("lnb_sb", [128, 8], F32)
        self.b1a_sb = S("b1a_sb", [128, NJ], F32)
        self.b1g_sb = S("b1g_sb", [128, NJ], F32)
        self.b2_sb = S("b2_sb", [128, 8], F32)
        self.gkw1_sb = S("gkw1_sb", [128, 8 * LR], BF16)
        self.gkw2_sb = S("gkw2_sb", [16, D], BF16)
        self.qs_row = S("qs_row", [1, DH], BF16)
        self.ks_row = S("ks_row", [1, DH], BF16)
        self.qsrep = S("qsrep", [128, DH], F32)
        self.ksrep = S("ksrep", [128, DH], F32)
        self.rott = S("rott", [128, 4 * 4 * DH], F32)
        self.st1 = S("st1", [1, CH], F32)
        self.st2 = S("st2", [1, CH], F32)
        self.st3 = S("st3", [1, CH], F32)
        self.rowa = S("rowa", [1, CH], BF16)
        self.rowb = S("rowb", [1, CH], BF16)
        self.colsA = S("colsA", [128, 32], F32)
        self.colsB = S("colsB", [128, 32], F32)
        self.invqk = S("invqk", [128, 32], F32)
        self.AR = S("arena", [128, 51 * 1024], BF16)
        rem = nc.sbuf_bytes_remaining
        assert rem >= 0, f"SBUF over budget by {-rem} bytes"
        self.sbuf_rem = rem

    def arena_bf(self, col_off_kb, n_cols):
        c0 = col_off_kb * 512
        return self.AR[:, c0:c0 + n_cols]

    def arena_f32(self, col_off_kb, n_cols):
        c0 = col_off_kb * 256
        return self.AR[:, :].bitcast(F32)[:, c0:c0 + n_cols]

    def _psum(self):
        nc = self.nc
        self.pb = []
        for i in range(8):
            t = nc.alloc_psum_tensor(f"pb{i}", [128, 512], F32)
            self.pb.append(Bank(self.p, t[:, :], f"b{i}"))

    def fence(self, eng):
        p = self.p
        semname = {"pe": "pew", "dve": "dvp", "act": "acp", "pool": "plp",
                   "sp": "spp"}[eng]
        if eng in ("dve", "pool"):
            p.emit(eng, lambda e: e.engine_nop(), inc=semname, tag="fence")
        elif eng == "act":
            p.emit(eng, lambda s: s.activation(self.acsp[:], self.acsp[:],
                                               AF.Identity),
                   inc=semname, tag="fence")
        else:
            p.emit(eng, lambda t: t.nop(), inc=semname, tag="fence")
        return p.mark(semname)

    def dma(self, queue, sem, fn, waits=(), tag=""):
        self.p.emit(queue, fn, waits=waits, inc=sem, inc_n=16, tag=tag)
        return self.p.mark(sem)

    def store_marks(self):
        return [self.p.mark(s) for s in ("stsp", "stz0", "stz1")]

    def mm_group(self, bank, mms, waits=(), tag="", starts=None, stops=None):
        p = self.p
        w = list(waits) + bank.open_round()
        n = len(mms)
        for i, fn in enumerate(mms):
            st = bank.start_flag() if starts is None else starts[i]
            sp = (i == n - 1) if stops is None else stops[i]
            p.emit("pe", (lambda t, fn=fn, st=st, sp=sp: fn(t, st, sp)),
                   waits=w if i == 0 else [],
                   inc="pew" if i == n - 1 else None, tag=f"{tag}[{i}]")
        m = p.mark("pew")
        bank.close_round(m)
        return m

    def load_constants(self):
        p = self.p
        p.emit("dve", lambda v: v.memset(self.acsp[:], 0.0), tag="acsp_init")
        p.emit("dve", lambda v: v.memset(self.colsA[:], 0.0), tag="ca_init")
        p.emit("dve", lambda v: v.memset(self.colsB[:], 0.0), inc="dvp",
               tag="cb_init")
        for sb, hb in [(self.ident, self.ident_f), (self.lt16, self.lt16_f),
                       (self.tri01, self.tri01_f),
                       (self.mask_first, self.mask_first_f),
                       (self.mask_last, self.mask_last_f),
                       (self.mask_neg, self.mask_neg_f),
                       (self.ones_col, self.ones_col_f),
                       (self.ones_row, self.ones_row_f),
                       (self.gkb_row, self.gkb_f)]:
            self.dma("pool", "ldc", lambda g, sb=sb, hb=hb: g.dma_start(
                out=sb[:], in_=hb[:, :]), tag="const")
        self.dma("pool", "ldc", lambda g: g.dma_start(
            out=self.rmsw_sb[:], in_=self.rmsw_f[:, :]), tag="rmsw")
        self.dma("pool", "ldc", lambda g: g.dma_start(
            out=self.fc[:], in_=self.fconsts_f[:, :]), tag="fconsts")
        self.dma("pool", "ldc", lambda g: g.dma_start(
            out=self.fccol[:], in_=self.fccol_f[:, :]), tag="fccol")
        self.m_fc = self.p.mark("ldc")
        self.dma("pool", "ldc", lambda g: g.dma_start(
            out=self.gkw1_sb[:, :].rearrange("p (k r) -> p k r", r=LR),
            in_=self.gkw1_f[:, :].rearrange("(k p) r -> p k r", p=128)), tag="gkw1")
        self.dma("pool", "ldc", lambda g: g.dma_start(
            out=self.gkw2_sb[:, :], in_=self.gkw2_f[:, :]), tag="gkw2")
        self.m_const = self.p.mark("ldc")
        for hf in range(2):
            b = self.pb[hf]
            m = self.mm_group(b, [lambda t, st, sp, hf=hf, b=b: t.matmul(
                b.ap[:, :], self.ones_row[:],
                self.gkb_row[0:1, hf * CH:(hf + 1) * CH], start=st, stop=sp)],
                waits=[self.m_const], tag=f"gkbrep{hf}")
            p.emit("dve", lambda v, hf=hf, b=b: v.tensor_copy(
                self.gkb_rep[:, hf * CH:(hf + 1) * CH], b.ap[:, :]),
                waits=b.r(), inc="dvp", tag=f"gkbev{hf}")
            b.read(p.mark("dvp"))

    def ln_stats(self, bf_src, f32_src, n_slices, bank_a, bank_b, deps, tag,
                 pre_cast_to=None):
        p = self.p
        bs1, bs2 = self.pb[5], self.pb[6]
        sqb_aps = [self._sqb_fn(j) for j in range(n_slices)]
        if pre_cast_to is not None:
            for j in range(n_slices):
                p.emit("dve", lambda v, j=j: v.tensor_copy(
                    pre_cast_to(j), f32_src(j)), waits=deps, tag=f"{tag}c{j}")
        for j in range(n_slices):
            p.emit("act", lambda s, j=j, sq=sqb_aps[j]: s.activation(
                sq, f32_src(j), AF.Square), waits=deps, tag=f"{tag}q{j}")
        m_cast = self.fence("dve") if pre_cast_to is not None else None
        m_sq = self.fence("act")
        self.mm_group(bs1, [
            (lambda t, st, sp, j=j: t.matmul(bs1.ap[0:1, :], self.ones_col[:],
                                             bf_src(j), start=st, stop=sp))
            for j in range(n_slices)],
            waits=[m_cast, self.m_const] + list(deps), tag=f"{tag}s1")
        m_stats = self.mm_group(bs2, [
            (lambda t, st, sp, j=j, sq=sqb_aps[j]: t.matmul(
                bs2.ap[0:1, :], self.ones_col[:], sq, start=st, stop=sp))
            for j in range(n_slices)], waits=[m_sq], tag=f"{tag}s2")
        nd = n_slices * 128
        p.emit("dve", lambda v: v.tensor_scalar_mul(self.st1[:], bs1.ap[0:1, :],
                                                    1.0 / nd),
               waits=[m_stats], tag=f"{tag}mean")
        p.emit("dve", lambda v: v.tensor_scalar_mul(self.st2[:], bs2.ap[0:1, :],
                                                    1.0 / nd), tag=f"{tag}ex2")
        p.emit("dve", lambda v: v.tensor_tensor(self.st3[:], self.st1[:],
                                                self.st1[:], ALU.mult), tag=f"{tag}m2")
        p.emit("dve", lambda v: v.tensor_sub(self.st2[:], self.st2[:], self.st3[:]),
               inc="dvp", tag=f"{tag}var")
        m_var = p.mark("dvp")
        bs1.read(m_var); bs2.read(m_var)
        p.emit("act", lambda s: s.activation(self.st3[:], self.st2[:], AF.Ln,
                                             bias=self.fc[0:1, 0:1]),
               waits=[m_var, self.m_fc], tag=f"{tag}ln")
        p.emit("act", lambda s: s.activation(self.st2[:], self.st3[:], AF.Exp,
                                             scale=-0.5), tag=f"{tag}exp")
        m_rstd = self.fence("act")
        p.emit("dve", lambda v: v.scalar_tensor_tensor(
            self.st3[:], self.st2[:], -1.0, self.st1[:], ALU.mult, ALU.mult),
            waits=[m_rstd], tag=f"{tag}negmr")
        p.emit("dve", lambda v: v.engine_nop(), tag="sp7")
        p.emit("dve", lambda v: v.tensor_copy(self.rowa[:], self.st2[:]),
               tag=f"{tag}ra")
        p.emit("dve", lambda v: v.tensor_copy(self.rowb[:], self.st3[:]),
               inc="dvp", tag=f"{tag}rb")
        m_rows = p.mark("dvp")
        self.mm_group(bank_a, [lambda t, st, sp: t.matmul(
            bank_a.ap[:, :], self.ones_row[:], self.rowa[0:1, :],
            start=st, stop=sp)], waits=[m_rows], tag=f"{tag}repa")
        m_rep = self.mm_group(bank_b, [lambda t, st, sp: t.matmul(
            bank_b.ap[:, :], self.ones_row[:], self.rowb[0:1, :],
            start=st, stop=sp)], waits=[m_rows], tag=f"{tag}repb")
        return m_rep

    def sqb(self, j):
        return self._sqb_fn(j)



def gla_views(kb):
    g = {}
    g["qT"] = kb.arena_bf(0, 4096)
    g["kT"] = kb.arena_bf(8, 4096)
    g["vtok"] = kb.arena_bf(16, 8192)
    g["gtok"] = kb.arena_bf(32, 4096)
    g["gate"] = kb.arena_bf(40, 8192)
    g["Sst"] = [kb.arena_f32(56 + 4 * h, 1024) for h in range(GH)]
    g["S0b"] = kb.arena_bf(72, 4096)
    g["QG"] = kb.arena_bf(80, 1024)
    g["KG"] = kb.arena_bf(82, 1024)
    g["KG2"] = kb.arena_bf(84, 1024)
    g["EGq"] = kb.arena_bf(86, 1024)
    g["EGk"] = kb.arena_bf(88, 1024)
    g["AT"] = kb.arena_bf(90, 512)
    g["kg2tok"] = kb.arena_bf(91, 1024)
    g["orms"] = kb.arena_bf(93, 2048)
    g["ubf"] = kb.arena_bf(97, 512)
    g["gtmp"] = kb.arena_f32(98, 512)
    g["gtmp2"] = kb.arena_f32(100, 512)
    g["xbf"] = kb.arena_bf(0, 4096)
    g["sqb"] = kb.arena_bf(8, 4096)
    return g


def gla_phase(kb):
    p = kb.p
    V = gla_views(kb)
    for h in range(GH):
        p.emit("dve", lambda v, h=h: v.memset(V["Sst"][h], 0.0), inc="dvp",
               tag=f"S0z{h}")
    kb.m_state_init = p.mark("dvp")
    for c in range(NCHUNK):
        gla_chunk(kb, V, c)


def gla_chunk(kb, V, c):
    p = kb.p
    pc = c % 2
    t0 = c * CH
    ych = kb.ych[pc]
    ws = kb.ws

    m_x = kb.dma("sp", f"ldy{pc}", lambda s: s.dma_start(
        out=ych[:, :].rearrange("p (j t) -> p j t", t=CH),
        in_=kb.x0[:, t0:t0 + CH].rearrange("(j p) t -> p j t", p=128)),
        waits=[kb.ych_free[pc]], tag=f"g{c}x")

    kb._sqb_fn = lambda j: V["sqb"][:, j * CH:(j + 1) * CH]
    xbf_j = lambda j: V["xbf"][:, j * CH:(j + 1) * CH]
    ych_j = lambda j: ych[:, j * CH:(j + 1) * CH]
    ba, bb = kb.pb[0], kb.pb[1]
    m_rep = kb.ln_stats(xbf_j, ych_j, 8, ba, bb,
                        [m_x, p.mark("dvp"), p.mark("pew")], f"g{c}L1",
                        pre_cast_to=xbf_j)
    for j in range(8):
        zt = kb.ztmp[j % 2]
        p.emit("dve", lambda v, j=j, zt=zt: v.tensor_tensor(
            zt[:], ych_j(j), ba.ap[:, :], ALU.mult),
            waits=[m_rep, m_x], tag=f"g{c}za{j}")
        p.emit("dve", lambda v, j=j, zt=zt: v.tensor_tensor(
            kb.zbf[:, j * CH:(j + 1) * CH], zt[:], bb.ap[:, :], ALU.add),
            tag=f"g{c}zb{j}")
    m_z = kb.fence("dve")
    ba.read(m_z); bb.read(m_z)

    for which, wf, dst in (("q", kb.wq_f, V["qT"]), ("k", kb.wk_f, V["kT"])):
        for grp in range(2):
            par = kb.ws_par = (getattr(kb, "ws_par", 0) + 1) % 2
            mld = kb.dma("pool", f"ldw{par}", lambda g, wf=wf, grp=grp, par=par: g.dma_start(
                out=ws[par][:, 0:4096].rearrange("p (k n) -> p k n", n=CH),
                in_=wf[:, grp * CH:(grp + 1) * CH].rearrange(
                    "(k p) n -> p k n", p=128)),
                waits=[kb.ws_free[par]], tag=f"g{c}w{which}{grp}")
            for fi in range(4):
                fm = grp * 4 + fi
                b = kb.pb[2 + (fi % 2)]
                mmw = kb.mm_group(b, [
                    (lambda t, st, sp, k=k, fi=fi, par=par, b=b: t.matmul(
                        b.ap[:, :],
                        ws[par][:, k * CH + fi * 128: k * CH + (fi + 1) * 128],
                        kb.zbf[:, k * CH:(k + 1) * CH], start=st, stop=sp))
                    for k in range(8)], waits=[m_z, mld], tag=f"g{c}{which}p{fm}")
                p.emit("dve", lambda v, fm=fm, b=b, dst=dst: v.tensor_copy(
                    dst[:, fm * CH:(fm + 1) * CH], b.ap[:, :]),
                    waits=b.r(), inc="dvp", tag=f"g{c}{which}e{fm}")
                b.read(p.mark("dvp"))
            kb.ws_free[par] = p.mark("dvp")
    m_qk = p.mark("dvp")

    for grp in range(4):
        par = kb.ws_par = (kb.ws_par + 1) % 2
        mld = kb.dma("pool", f"ldw{par}", lambda g, grp=grp, par=par: g.dma_start(
            out=ws[par][:, 0:4096].rearrange("p (k n) -> p k n", n=CH),
            in_=kb.wv_f[:, grp * CH:(grp + 1) * CH].rearrange(
                "(k p) n -> p k n", p=128)),
            waits=[kb.ws_free[par]], tag=f"g{c}wv{grp}")
        for m in range(NT):
            b = kb.pb[4 + (m % 2)]
            kb.mm_group(b, [
                (lambda t, st, sp, k=k, m=m, par=par, b=b: t.matmul(
                    b.ap[:, :],
                    kb.zbf[:, k * CH + m * 128: k * CH + (m + 1) * 128],
                    ws[par][:, k * CH:(k + 1) * CH], start=st, stop=sp))
                for k in range(8)], waits=[m_z, mld], tag=f"g{c}v{grp}_{m}")
            p.emit("dve", lambda v, m=m, grp=grp, b=b: v.tensor_copy(
                V["vtok"][:, m * DV + grp * CH: m * DV + (grp + 1) * CH],
                b.ap[:, :]), waits=b.r(), inc="dvp", tag=f"g{c}ve{grp}_{m}")
            b.read(p.mark("dvp"))
        kb.ws_free[par] = p.mark("dvp")
    m_v = kb.fence("dve")

    for grp in range(4):
        par = kb.ws_par = (kb.ws_par + 1) % 2
        mld = kb.dma("pool", f"ldw{par}", lambda g, grp=grp, par=par: g.dma_start(
            out=ws[par][:, 0:4096].rearrange("p (k n) -> p k n", n=CH),
            in_=kb.wg_f[:, grp * CH:(grp + 1) * CH].rearrange(
                "(k p) n -> p k n", p=128)),
            waits=[kb.ws_free[par]], tag=f"g{c}wg{grp}")
        for fi in range(4):
            fm = grp * 4 + fi
            b = kb.pb[4 + (fi % 2)]
            kb.mm_group(b, [
                (lambda t, st, sp, k=k, fi=fi, par=par, b=b: t.matmul(
                    b.ap[:, :],
                    ws[par][:, k * CH + fi * 128: k * CH + (fi + 1) * 128],
                    kb.zbf[:, k * CH:(k + 1) * CH], start=st, stop=sp))
                for k in range(8)], waits=[m_z, mld], tag=f"g{c}gp{fm}")
            p.emit("act", lambda s, fm=fm, b=b: s.activation(
                V["gate"][:, fm * CH:(fm + 1) * CH], b.ap[:, :], AF.Silu),
                waits=b.r(), inc="acp", tag=f"g{c}ge{fm}")
            b.read(p.mark("acp"))
        kb.ws_free[par] = p.mark("acp")
    kb.m_gate = kb.fence("act")

    b = kb.pb[2]
    kb.mm_group(b, [
        (lambda t, st, sp, k=k, b=b: t.matmul(
            b.ap[0:16, :], kb.gkw1_sb[:, k * LR:(k + 1) * LR],
            kb.zbf[:, k * CH:(k + 1) * CH], start=st, stop=sp))
        for k in range(8)], waits=[m_z, kb.m_const], tag=f"g{c}u")
    p.emit("dve", lambda v, b=b: v.tensor_copy(V["ubf"][0:16, 0:CH],
                                               b.ap[0:16, :]),
           waits=b.r(), inc="dvp", tag=f"g{c}ue")
    b.read(p.mark("dvp"))
    m_u = kb.fence("dve")
    m_gm = []
    for m in range(NT):
        for half in range(2):
            bk = kb.pb[2 + half]
            kb.mm_group(bk, [lambda t, st, sp, m=m, half=half, bk=bk: t.matmul(
                bk.ap[:, :], V["ubf"][0:16, m * 128:(m + 1) * 128],
                kb.gkw2_sb[0:16, half * CH:(half + 1) * CH], start=st, stop=sp)],
                waits=[m_u, kb.m_const], tag=f"g{c}gl{m}_{half}")
            hs = slice(half * CH, (half + 1) * CH)
            p.emit("dve", lambda v, bk=bk, hs=hs: v.tensor_tensor(
                V["gtmp"][:, :], bk.ap[:, :], kb.gkb_rep[:, hs], ALU.add),
                waits=bk.r(), inc="dvp", tag=f"g{c}g0_{m}{half}")
            bk.read(p.mark("dvp"))
            m_abs = p.mark("dvp")
            p.emit("act", lambda s: s.activation(V["gtmp2"][:, :], V["gtmp"][:, :],
                                                 AF.Abs), waits=[m_abs],
                   tag=f"g{c}gA{m}{half}")
            p.emit("act", lambda s: s.activation(V["gtmp2"][:, :], V["gtmp2"][:, :],
                                                 AF.Exp, scale=-1.0),
                   tag=f"g{c}gE{m}{half}")
            p.emit("act", lambda s: s.activation(V["gtmp2"][:, :], V["gtmp2"][:, :],
                                                 AF.Ln, bias=kb.fccol[:, 1:2]),
                   inc="acp", tag=f"g{c}gL{m}{half}")
            m_l = p.mark("acp")
            p.emit("dve", lambda v: v.tensor_scalar_min(
                V["gtmp"][:, :], V["gtmp"][:, :], 0.0),
                waits=[m_l], tag=f"g{c}gM{m}{half}")
            p.emit("dve", lambda v, m=m, half=half: v.tensor_sub(
                V["gtok"][:, m * D + half * CH: m * D + (half + 1) * CH],
                V["gtmp"][:, :], V["gtmp2"][:, :]), inc="dvp",
                tag=f"g{c}gO{m}{half}")
        m_gm.append(kb.fence("dve"))

    for m in range(NT):
        gla_scan_tile(kb, V, c, m, m_gm[m], m_qk, m_v)

    gla_out(kb, V, c, pc)


def gla_scan_tile(kb, V, c, m, m_g, m_qk, m_v):
    p = kb.p
    gt = c * NT + m
    b01 = [kb.pb[0], kb.pb[1]]
    for bi in range(2):
        bk = b01[bi]
        kb.mm_group(bk, [
            (lambda t, st, sp, j=j, bk=bk: t.matmul(
                bk.ap[:, (j % 4) * 128:((j % 4) + 1) * 128],
                V["gtok"][:, m * D + j * 128: m * D + (j + 1) * 128],
                kb.lt16[:], start=st, stop=sp))
            for j in range(bi * 4, bi * 4 + 4)],
            waits=[m_g, kb.m_const], tag=f"s{gt}G{bi}")
    for j in range(8):
        bk = b01[j // 4]
        col = (j % 4) * 128
        p.emit("dve", lambda v, j=j, bk=bk, col=col: v.tensor_copy(
            kb.colsA[:, j:j + 1], bk.ap[:, col + 63:col + 64]),
            waits=bk.r() + [p.mark("acp")], tag=f"s{gt}pg{j}")
        p.emit("dve", lambda v, j=j, bk=bk, col=col: v.tensor_scalar(
            kb.colsA[:, 8 + j:9 + j], bk.ap[:, col + 63:col + 64], -1.0, None,
            ALU.mult), tag=f"s{gt}ng{j}")
        p.emit("dve", lambda v, j=j, bk=bk, col=col: v.tensor_scalar(
            kb.colsA[:, 16 + j:17 + j], bk.ap[:, col + 63:col + 64], -1.0,
            -LN16, ALU.mult, ALU.add), tag=f"s{gt}nl{j}")
    m_cols = kb.fence("dve")
    for j in range(8):
        bk = b01[j // 4]
        col = (j % 4) * 128
        p.emit("act", lambda s, j=j, bk=bk, col=col: s.activation(
            V["EGq"][:, j * 128:(j + 1) * 128], bk.ap[:, col:col + 128],
            AF.Exp, bias=kb.colsA[:, 16 + j:17 + j]), waits=[m_cols],
            tag=f"s{gt}eq{j}")
        p.emit("act", lambda s, j=j, bk=bk, col=col: s.activation(
            V["EGk"][:, j * 128:(j + 1) * 128], bk.ap[:, col:col + 128],
            AF.Exp, scale=-1.0, bias=kb.colsA[:, j:j + 1]), tag=f"s{gt}ek{j}")
        p.emit("act", lambda s, j=j, bk=bk, col=col: s.activation(
            kb.colsB[:, j:j + 1], bk.ap[:, col + 63:col + 64], AF.Exp),
            tag=f"s{gt}em{j}")
        p.emit("act", lambda s, j=j, bk=bk, col=col: s.activation(
            kb.colsB[:, 8 + j:9 + j], bk.ap[:, col + 127:col + 128], AF.Exp,
            bias=kb.colsA[:, 8 + j:9 + j]), tag=f"s{gt}ed{j}")
        p.emit("act", lambda s, j=j, bk=bk, col=col: s.activation(
            kb.colsB[:, 16 + j:17 + j], bk.ap[:, col + 127:col + 128], AF.Exp),
            tag=f"s{gt}e2{j}")
    m_eg = kb.fence("act")
    b01[0].read(m_eg); b01[1].read(m_eg)
    for j in range(8):
        p.emit("dve", lambda v, j=j: v.tensor_mul(
            V["QG"][:, j * 128:(j + 1) * 128],
            V["qT"][:, j * CH + m * 128: j * CH + (m + 1) * 128],
            V["EGq"][:, j * 128:(j + 1) * 128]),
            waits=[m_eg, m_qk, p.mark("pew")], tag=f"s{gt}qg{j}")
        p.emit("dve", lambda v, j=j: v.tensor_mul(
            V["KG"][:, j * 128:(j + 1) * 128],
            V["kT"][:, j * CH + m * 128: j * CH + (m + 1) * 128],
            V["EGk"][:, j * 128:(j + 1) * 128]), tag=f"s{gt}kg{j}")
    m_qgkg = kb.fence("dve")
    for j in range(8):
        p.emit("dve", lambda v, j=j: v.tensor_scalar(
            V["KG2"][:, j * 128:(j + 1) * 128],
            V["KG"][:, j * 128:(j + 1) * 128],
            kb.colsB[:, 8 + j:9 + j], None, ALU.mult),
            waits=[m_qgkg], tag=f"s{gt}k2_{j}")
    m_kg2 = kb.fence("dve")
    bat = kb.pb[2]
    kb.mm_group(bat, [
        (lambda t, st, sp, h=h, kf=kf: t.matmul(
            bat.ap[:, h * 128:(h + 1) * 128],
            V["KG"][:, (h * 2 + kf) * 128:(h * 2 + kf + 1) * 128],
            V["QG"][:, (h * 2 + kf) * 128:(h * 2 + kf + 1) * 128],
            start=st, stop=sp))
        for h in range(GH) for kf in range(2)],
        waits=[m_qgkg], tag=f"s{gt}at")
    for h in range(GH):
        p.emit("dve", lambda v, h=h: v.tensor_mul(
            V["AT"][:, h * 128:(h + 1) * 128],
            bat.ap[:, h * 128:(h + 1) * 128], kb.tri01[:]),
            waits=bat.r() + [kb.m_const], tag=f"s{gt}am{h}")
    m_at = kb.fence("dve")
    bat.read(m_at)
    for h in range(GH):
        for kf in range(2):
            j = h * 2 + kf
            p.emit("dve", lambda v, h=h, kf=kf, j=j: v.tensor_scalar(
                V["S0b"][:, j * HV:(j + 1) * HV],
                V["Sst"][h][:, kf * HV:(kf + 1) * HV],
                kb.colsB[:, j:j + 1], None, ALU.mult),
                waits=[m_eg, kb.m_state_init], tag=f"s{gt}s0{j}")
    m_s0 = kb.fence("dve")
    if kb.debug and c == 0 and m == 0:
        kb.dma("pool", "ldc", lambda g: g.dma_start(
            out=kb.dbg[0][:, 0:32], in_=kb.colsA[:, 0:32]),
            waits=[m_eg], tag="dbg0")
        kb.dma("pool", "ldc", lambda g: g.dma_start(
            out=kb.dbg[0][:, 32:64], in_=kb.colsB[:, 0:32]),
            waits=[m_eg], tag="dbg0b")
        kb.dma("pool", "ldc", lambda g: g.dma_start(
            out=kb.dbg[1][:, 0:1024], in_=V["QG"][:, :]),
            waits=[m_qgkg], tag="dbg1")
        kb.dma("pool", "ldc", lambda g: g.dma_start(
            out=kb.dbg[2][:, 0:1024],
            in_=V["EGk"][:, :]), waits=[m_eg], tag="dbg2")
        kb.dma("pool", "ldc", lambda g: g.dma_start(
            out=kb.dbg[3][:, 0:1024], in_=V["EGq"][:, :]),
            waits=[m_eg], tag="dbg3")
        m_dbg = p.mark("ldc")
        p.emit("dve", lambda v: v.engine_nop(), waits=[m_dbg], tag="dbgstall")
        p.emit("act", lambda s: s.activation(kb.acsp[:], kb.acsp[:],
                                             AF.Identity),
               waits=[m_dbg], tag="dbgstall2")
    obk = [kb.pb[3], kb.pb[4], kb.pb[0], kb.pb[1]]
    for h in range(GH):
        bo = obk[h]
        mms = []
        for vf in range(4):
            mms.append(lambda t, st, sp, h=h, vf=vf, bo=bo: t.matmul(
                bo.ap[:, vf * 128:(vf + 1) * 128],
                V["vtok"][:, m * DV + h * HV + vf * 128:
                          m * DV + h * HV + (vf + 1) * 128],
                V["AT"][:, h * 128:(h + 1) * 128], start=st, stop=sp))
        for vf in range(4):
            for kf in range(2):
                j = h * 2 + kf
                mms.append(lambda t, st, sp, h=h, vf=vf, kf=kf, j=j, bo=bo:
                           t.matmul(
                    bo.ap[:, vf * 128:(vf + 1) * 128],
                    V["S0b"][:, j * HV + vf * 128: j * HV + (vf + 1) * 128],
                    V["QG"][:, j * 128:(j + 1) * 128], start=st, stop=sp))
        kb.mm_group(bo, mms, waits=[m_at, m_s0, m_v], tag=f"s{gt}o{h}")
    brs = kb.pb[5]
    for h in range(GH):
        bo = obk[h]
        p.emit("act", lambda s, h=h, bo=bo: s.activation(
            kb.osq[:, h * CH:(h + 1) * CH], bo.ap[:, :], AF.Square),
            waits=bo.r(), tag=f"s{gt}sq{h}")
    m_osq = kb.fence("act")
    kb.mm_group(brs, [
        (lambda t, st, sp, h=h, vf=vf: t.matmul(
            brs.ap[0:1, h * 128:(h + 1) * 128], kb.ones_col[:],
            kb.osq[:, h * CH + vf * 128: h * CH + (vf + 1) * 128],
            start=st, stop=sp))
        for h in range(GH) for vf in range(4)], waits=[m_osq], tag=f"s{gt}rs")
    p.emit("act", lambda s: s.activation(kb.st1[:], brs.ap[0:1, :], AF.Sqrt,
                                         scale=1.0 / HV, bias=kb.fc[0:1, 0:1]),
           waits=brs.r() + [kb.m_fc], tag=f"s{gt}rq")
    m_rq = kb.fence("act")
    brs.read(m_rq)
    p.emit("dve", lambda v: v.reciprocal(kb.st2[:], kb.st1[:]),
           waits=[m_rq], tag=f"s{gt}rc")
    p.emit("dve", lambda v: v.engine_nop(), tag="sp6")
    p.emit("dve", lambda v: v.tensor_copy(kb.rowa[:], kb.st2[:]),
           tag=f"s{gt}rw")
    m_rrow = kb.fence("dve")
    brr = kb.pb[6]
    kb.mm_group(brr, [lambda t, st, sp: t.matmul(
        brr.ap[:, :], kb.ones_row[:], kb.rowa[0:1, :], start=st, stop=sp)],
        waits=[m_rrow], tag=f"s{gt}rp")
    p.emit("dve", lambda v: v.tensor_copy(kb.rrep_sb[:, :], brr.ap[:, :]),
           waits=brr.r(), inc="dvp", tag=f"s{gt}re")
    brr.read(p.mark("dvp"))
    m_rrep = p.mark("dvp")
    for h in range(GH):
        bo = obk[h]
        for vf in range(4):
            p.emit("dve", lambda v, h=h, vf=vf, bo=bo: v.scalar_tensor_tensor(
                V["orms"][:, (h * 4 + vf) * 128:(h * 4 + vf + 1) * 128],
                bo.ap[:, vf * 128:(vf + 1) * 128],
                kb.rmsw_sb[:, vf:vf + 1],
                kb.rrep_sb[:, h * 128:(h + 1) * 128],
                ALU.mult, ALU.mult),
                waits=[m_rrep], tag=f"s{gt}om{h}{vf}")
    m_orms = kb.fence("dve")
    for h in range(GH):
        obk[h].read(m_orms)
    for f in range(16):
        p.emit("dve", lambda v, f=f: v.tensor_mul(
            V["gate"][:, f * CH + m * 128: f * CH + (m + 1) * 128],
            V["orms"][:, f * 128:(f + 1) * 128],
            V["gate"][:, f * CH + m * 128: f * CH + (m + 1) * 128]),
            waits=[m_orms, kb.m_gate], tag=f"s{gt}og{f}")
    kb.m_ogated = kb.fence("dve")
    ptb = kb.pb[7]
    pt_ap = ptb.ap.bitcast(BF16)
    kb.mm_group(ptb, [
        (lambda t, st, sp, j=j: t.transpose(
            pt_ap[:, j * 128:(j + 1) * 128],
            V["KG2"][:, j * 128:(j + 1) * 128], kb.ident[:]))
        for j in range(8)], waits=[m_kg2, kb.m_const], tag=f"s{gt}kt")
    p.emit("dve", lambda v: v.tensor_copy(V["kg2tok"][:, :], pt_ap[:, :]),
           waits=ptb.r(), inc="dvp", tag=f"s{gt}ke")
    ptb.read(p.mark("dvp"))
    m_kgt = kb.fence("dve")
    for h in range(GH):
        for kf in range(2):
            j = h * 2 + kf
            bd = kb.pb[2 + kf]
            kb.mm_group(bd, [lambda t, st, sp, h=h, j=j, bd=bd: t.matmul(
                bd.ap[:, :], V["kg2tok"][:, j * 128:(j + 1) * 128],
                V["vtok"][:, m * DV + h * HV: m * DV + (h + 1) * HV],
                start=st, stop=sp)], waits=[m_kgt, m_v], tag=f"s{gt}sd{h}{kf}")
            p.emit("dve", lambda v, h=h, kf=kf, j=j, bd=bd: v.scalar_tensor_tensor(
                V["Sst"][h][:, kf * HV:(kf + 1) * HV],
                V["Sst"][h][:, kf * HV:(kf + 1) * HV],
                kb.colsB[:, 16 + j:17 + j], bd.ap[:, :], ALU.mult, ALU.add),
                waits=bd.r(), inc="dvp", tag=f"s{gt}su{h}{kf}")
            bd.read(p.mark("dvp"))


def gla_out(kb, V, c, pc):
    p = kb.p
    t0 = c * CH
    ych = kb.ych[pc]
    ws = kb.ws
    dmw = []
    for dm in range(8):
        dmw += kb.pb[dm].open_round()
    for gi in range(4):
        par = kb.ws_par = (kb.ws_par + 1) % 2
        mld = kb.dma("pool", f"ldw{par}", lambda g, gi=gi, par=par: g.dma_start(
            out=ws[par][:, 0:4096].rearrange("p (k n) -> p k n", n=1024),
            in_=kb.wo_gla_f[gi * 512:(gi + 1) * 512, :].rearrange(
                "(k p) n -> p k n", p=128)),
            waits=[kb.ws_free[par]], tag=f"g{c}wo{gi}")
        for dm in range(8):
            bo = kb.pb[dm]
            for kk in range(4):
                kglob = gi * 4 + kk
                st = bo.start_flag()
                last_in_grp = (dm == 7 and kk == 3)
                w = []
                if dm == 0 and kk == 0:
                    w = [mld] + (dmw + [kb.m_ogated] if gi == 0 else [])
                p.emit("pe", lambda t, dm=dm, kk=kk, kglob=kglob, par=par,
                       st=st, bo=bo: t.matmul(
                    bo.ap[:, :],
                    ws[par][:, kk * 1024 + dm * 128: kk * 1024 + (dm + 1) * 128],
                    V["gate"][:, kglob * CH:(kglob + 1) * CH],
                    start=st, stop=(kglob == 15)),
                    waits=w, inc="pew" if last_in_grp else None,
                    tag=f"g{c}WO{gi}_{dm}_{kk}")
        kb.ws_free[par] = p.mark("pew")
    m_wo = p.mark("pew")
    for dm in range(8):
        kb.pb[dm].close_round(m_wo)
    for dm in range(8):
        bo = kb.pb[dm]
        p.emit("dve", lambda v, dm=dm, bo=bo: v.tensor_copy(
            kb.gobf[:, dm * CH:(dm + 1) * CH], bo.ap[:, :]),
            waits=bo.r(), inc="dvp", tag=f"g{c}goe{dm}")
        bo.read(p.mark("dvp"))
    m_go = kb.fence("dve")
    gobf_j = lambda j: kb.gobf[:, j * CH:(j + 1) * CH]
    ba, bb = kb.pb[0], kb.pb[1]
    m_rep2 = kb.ln_stats(gobf_j, gobf_j, 8, ba, bb,
                         [m_go, p.mark("dvp"), p.mark("pew")], f"g{c}L2")
    for j in range(8):
        zt = kb.ztmp[j % 2]
        p.emit("dve", lambda v, j=j, zt=zt: v.tensor_tensor(
            zt[:], gobf_j(j), ba.ap[:, :], ALU.mult),
            waits=[m_rep2], tag=f"g{c}xa{j}")
        p.emit("dve", lambda v, zt=zt: v.tensor_tensor(
            zt[:], zt[:], bb.ap[:, :], ALU.add), tag=f"g{c}xb{j}")
        p.emit("dve", lambda v, j=j, zt=zt: v.tensor_tensor(
            ych[:, j * CH:(j + 1) * CH], ych[:, j * CH:(j + 1) * CH],
            zt[:], ALU.add), tag=f"g{c}xc{j}")
    m_x1 = kb.fence("dve")
    ba.read(m_x1); bb.read(m_x1)
    kb.dma("sp", "stsp", lambda s: s.dma_start(
        out=kb.xbuf[:, t0:t0 + CH].rearrange("(j p) t -> p j t", p=128),
        in_=ych[:, :].rearrange("p (j t) -> p j t", t=CH)),
        waits=[m_x1], tag=f"g{c}x1st")
    xbf_j = lambda j: V["xbf"][:, j * CH:(j + 1) * CH]
    ych_j = lambda j: ych[:, j * CH:(j + 1) * CH]
    m_rep3 = kb.ln_stats(xbf_j, ych_j, 8, ba, bb,
                         [m_x1, p.mark("pew")], f"g{c}L3", pre_cast_to=xbf_j)
    st_marks = [None, None]
    for j in range(8):
        zt = kb.ztmp[j % 2]
        p.emit("dve", lambda v, j=j, zt=zt: v.tensor_tensor(
            zt[:], ych_j(j), ba.ap[:, :], ALU.mult),
            waits=[m_rep3, st_marks[j % 2]], tag=f"g{c}ya{j}")
        p.emit("dve", lambda v, zt=zt: v.tensor_tensor(
            zt[:], zt[:], bb.ap[:, :], ALU.add), tag=f"g{c}yb{j}")
        m_j = kb.fence("dve")
        st_marks[j % 2] = kb.dma("sp", f"stz{j % 2}",
            lambda s, j=j, zt=zt: s.dma_start(
                out=kb.ybuf[0][j * 128:(j + 1) * 128, t0:t0 + CH], in_=zt[:]),
            waits=[m_j], tag=f"g{c}y0st{j}")
    m_y0 = kb.fence("dve")
    ba.read(m_y0); bb.read(m_y0)
    kb.ych_free[pc] = m_y0


def attn_views(kb):
    a = {}
    a["qtok"] = kb.arena_bf(0, 4096)
    a["ktok"] = kb.arena_bf(8, 4096)
    a["vring"] = kb.arena_bf(16, 8192)
    a["kTh"] = [kb.AR[0:64, 32 * 512 + h * 768: 32 * 512 + (h + 1) * 768]
                for h in range(AH)]
    a["qTh"] = [kb.AR[0:64, 56 * 512 + h * 512: 56 * 512 + (h + 1) * 512]
                for h in range(AH)]
    a["aoT"] = kb.arena_bf(72, 4096)
    a["atb"] = kb.arena_bf(80, 384)
    return a


def attn_layer(kb, L):
    p = kb.p
    V = attn_views(kb)
    y_in, y_out = kb.ybuf[0], kb.ybuf[1]
    gate_w = [p.mark("pew"), p.mark("dvp"), p.mark("acp")]
    mq = kb.dma("pool", "lpar", lambda g: g.dma_start(
        out=kb.qs_row[:], in_=kb.qs_f[L, :, :]), waits=gate_w, tag=f"a{L}qs")
    mk = kb.dma("pool", "lpar", lambda g: g.dma_start(
        out=kb.ks_row[:], in_=kb.ks_f[L, :, :]), waits=gate_w, tag=f"a{L}ks")
    for which, row, rep, mm, scale in (("q", kb.qs_row, kb.qsrep, mq, 8.0),
                                       ("k", kb.ks_row, kb.ksrep, mk, 1.0)):
        bk = kb.pb[6]
        kb.mm_group(bk, [lambda t, st, sp, row=row: t.matmul(
            bk.ap[:, 0:DH], kb.ones_row[:], row[0:1, :], start=st, stop=sp)],
            waits=[mm, kb.m_const], tag=f"a{L}{which}rep")
        p.emit("dve", lambda v, rep=rep, bk=bk, scale=scale: v.tensor_scalar_mul(
            rep[:, :], bk.ap[:, 0:DH], scale), waits=bk.r(), inc="dvp",
            tag=f"a{L}{which}repe")
        bk.read(p.mark("dvp"))
    for h in range(AH):
        p.emit("dve", lambda v, h=h: v.memset(V["kTh"][h][:, 0:256], 0.0),
               waits=[p.mark("pew")], tag=f"a{L}krz{h}")
    p.emit("dve", lambda v: v.memset(V["vring"][:, 4096 + 2 * 1024: 8192], 0.0),
           waits=[p.mark("pew")], inc="dvp", tag=f"a{L}vrz")
    kb.m_ring0 = p.mark("dvp")
    kb.m_ringcopy = kb.m_ring0
    for c in range(NCHUNK):
        attn_chunk(kb, V, L, c, y_in, y_out)


def attn_chunk(kb, V, L, c, y_in, y_out):
    p = kb.p
    pc = c % 2
    t0 = c * CH
    ych = kb.ych[pc]
    ysh = kb.ysh[pc]
    ws = kb.ws

    m_y = kb.dma("sp", f"ldy{pc}", lambda s: s.dma_start(
        out=ych[:, :].rearrange("p (j t) -> p j t", t=CH),
        in_=y_in[:, t0:t0 + CH].rearrange("(j p) t -> p j t", p=128)),
        waits=[kb.ych_free[pc]], tag=f"a{L}c{c}y")
    st_now = kb.store_marks()
    if c == 0:
        for j in range(4):
            p.emit("dve", lambda v, j=j, ysh=ysh: v.memset(
                ysh[:, j * CH:j * CH + 1], 0.0),
                waits=[kb.ysh_free[pc]], inc="dvp", tag=f"a{L}c0z{j}")
        kb.dma("pool", f"ldsh{pc}", lambda g: g.dma_start(
            out=ysh[:, :].rearrange("p (j t) -> p j t", t=CH)[:, :, 1:CH],
            in_=y_in[512:1024, 0:CH - 1].rearrange("(j p) t -> p j t", p=128)),
            waits=[kb.ysh_free[pc]] + st_now, tag=f"a{L}c{c}ys")
        m_ysh = [p.mark(f"ldsh{pc}"), p.mark("dvp")]
    else:
        m1 = kb.dma("pool", f"ldsh{pc}", lambda g: g.dma_start(
            out=ysh[:, :].rearrange("p (j t) -> p j t", t=CH),
            in_=y_in[512:1024, t0 - 1:t0 + CH - 1].rearrange(
                "(j p) t -> p j t", p=128)),
            waits=[kb.ysh_free[pc]] + st_now, tag=f"a{L}c{c}ys")
        m_ysh = [m1]
    m_dvp_now = p.mark("dvp")
    for cid, tf in enumerate((kb.rot_cs_f, kb.rot_ssx_f, kb.rot_ci_f,
                              kb.rot_six_f)):
        kb.dma("sp", "ldr", lambda s, cid=cid, tf=tf: s.dma_start(
            out=kb.rott[:, cid * 256:(cid + 1) * 256].rearrange(
                "p (m e) -> p m e", e=DH),
            in_=tf[t0:t0 + CH, :].rearrange("(m p) e -> p m e", p=128)),
            waits=[m_dvp_now], tag=f"a{L}c{c}rt{cid}")
    m_rot = p.mark("ldr")

    for j in range(4):
        p.emit("dve", lambda v, j=j: v.tensor_copy(
            kb.zbf[:, j * CH:(j + 1) * CH], ych[:, j * CH:(j + 1) * CH]),
            waits=[m_y, p.mark("pew")], tag=f"a{L}c{c}lc{j}")
    m_low = kb.fence("dve")

    def ysb(k):
        if k < 4:
            return kb.zbf[:, k * CH:(k + 1) * CH]
        return ysh[:, (k - 4) * CH:(k - 3) * CH]

    m_evacs = None
    for grp in range(6):
        par = kb.ws_par = (kb.ws_par + 1) % 2
        mld = kb.dma("pool", f"ldw{par}", lambda g, grp=grp, par=par: g.dma_start(
            out=ws[par][:, 0:4096].rearrange("p (k n) -> p k n", n=CH),
            in_=kb.wqkv_f[L, :, grp * CH:(grp + 1) * CH].rearrange(
                "(k p) n -> p k n", p=128)),
            waits=[kb.ws_free[par]], tag=f"a{L}c{c}wqkv{grp}")
        for m in range(NT):
            b = kb.pb[2 + (m % 2)]
            kb.mm_group(b, [
                (lambda t, st, sp, k=k, m=m, par=par, b=b: t.matmul(
                    b.ap[:, :],
                    ysb(k)[:, m * 128:(m + 1) * 128],
                    ws[par][:, k * CH:(k + 1) * CH], start=st, stop=sp))
                for k in range(8)],
                waits=[m_low, mld] + m_ysh, tag=f"a{L}c{c}p{grp}_{m}")
            if grp < 2:
                dst = V["qtok"][:, m * 1024 + grp * CH: m * 1024 + (grp + 1) * CH]
            elif grp < 4:
                dst = V["ktok"][:, m * 1024 + (grp - 2) * CH:
                                m * 1024 + (grp - 1) * CH]
            else:
                dst = V["vring"][:, pc * 4096 + m * 1024 + (grp - 4) * CH:
                                 pc * 4096 + m * 1024 + (grp - 3) * CH]
            p.emit("dve", lambda v, b=b, dst=dst: v.tensor_copy(dst, b.ap[:, :]),
                   waits=b.r() + ([kb.m_ringcopy] if grp >= 4 else []),
                   inc="dvp", tag=f"a{L}c{c}pe{grp}_{m}")
            b.read(p.mark("dvp"))
        kb.ws_free[par] = p.mark("dvp")
    m_qkv = kb.fence("dve")

    m_rotdone = None
    for m in range(NT):
        qm = V["qtok"][:, m * 1024:(m + 1) * 1024]
        km = V["ktok"][:, m * 1024:(m + 1) * 1024]
        sq_q = kb.osq[:, 0:1024]
        sq_k = kb.osq[:, 1024:2048]
        p.emit("act", lambda s, qm=qm, sq_q=sq_q: s.activation(
            sq_q, qm, AF.Square), waits=[m_qkv, m_rotdone],
            tag=f"a{L}c{c}sqq{m}")
        p.emit("act", lambda s, km=km, sq_k=sq_k: s.activation(
            sq_k, km, AF.Square), tag=f"a{L}c{c}sqk{m}")
        m_sq = kb.fence("act")
        p.emit("dve", lambda v, sq_q=sq_q: v.reduce_sum(
            kb.invqk[:, 0:16], sq_q.rearrange("p (h e) -> p h e", e=DH),
            axis=AX.X), waits=[m_sq], tag=f"a{L}c{c}ssq{m}")
        p.emit("dve", lambda v, sq_k=sq_k: v.reduce_sum(
            kb.invqk[:, 16:32], sq_k.rearrange("p (h e) -> p h e", e=DH),
            axis=AX.X), inc="dvp", tag=f"a{L}c{c}ssk{m}")
        m_ss = p.mark("dvp")
        p.emit("act", lambda s: s.activation(kb.invqk[:, 0:32], kb.invqk[:, 0:32],
                                             AF.Sqrt), waits=[m_ss],
               tag=f"a{L}c{c}nsq{m}")
        m_nsq = kb.fence("act")
        p.emit("dve", lambda v: v.tensor_single_scalar(
            kb.invqk[:, 0:32], kb.invqk[:, 0:32], 1e-12, ALU.max),
            waits=[m_nsq], tag=f"a{L}c{c}nmx{m}")
        p.emit("dve", lambda v: v.engine_nop(), tag="sp1")
        p.emit("dve", lambda v: v.engine_nop(), tag="sp2")
        p.emit("dve", lambda v: v.reciprocal(kb.invqk[:, 0:32], kb.invqk[:, 0:32]),
               tag=f"a{L}c{c}nrc{m}")
        p.emit("dve", lambda v: v.engine_nop(), tag="sp3")
        p.emit("dve", lambda v: v.engine_nop(), inc="dvp", tag="sp4")
        m_inv = p.mark("dvp")
        for which, tok, io, rep in (("q", qm, 0, kb.qsrep), ("k", km, 16, kb.ksrep)):
            p.emit("dve", lambda v, tok=tok, io=io: v.tensor_mul(
                tok.rearrange("p (h e) -> p h e", e=DH),
                tok.rearrange("p (h e) -> p h e", e=DH),
                kb.invqk[:, io:io + 16].unsqueeze(2).broadcast_to([128, 16, DH])),
                waits=[m_inv], tag=f"a{L}c{c}{which}n{m}")
            p.emit("dve", lambda v, tok=tok, rep=rep: v.tensor_mul(
                tok.rearrange("p (h e) -> p h e", e=DH),
                tok.rearrange("p (h e) -> p h e", e=DH),
                rep[:, :].unsqueeze(1).broadcast_to([128, 16, DH])),
                tag=f"a{L}c{c}{which}s{m}")
        for which, tok, tmp, cid, sid in (("q", qm, kb.osq[:, 0:1024], 0, 1),
                                          ("k", km, kb.osq[:, 1024:2048], 2, 3)):
            cosv = kb.rott[:, (cid * 4 + m) * DH:(cid * 4 + m + 1) * DH]
            sinv = kb.rott[:, (sid * 4 + m) * DH:(sid * 4 + m + 1) * DH]
            tokv = tok.rearrange("p (h x e) -> p h x e", x=2, e=32)
            tmpv = tmp.rearrange("p (h x e) -> p h x e", x=2, e=32)
            p.emit("dve", lambda v, tmpv=tmpv, tokv=tokv, sinv=sinv: v.tensor_mul(
                tmpv[:, :, 0, :], tokv[:, :, 1, :],
                sinv[:, 0:32].unsqueeze(1).broadcast_to([128, 16, 32])),
                waits=[m_rot], tag=f"a{L}c{c}{which}r1{m}")
            p.emit("dve", lambda v, tmpv=tmpv, tokv=tokv, sinv=sinv: v.tensor_mul(
                tmpv[:, :, 1, :], tokv[:, :, 0, :],
                sinv[:, 32:64].unsqueeze(1).broadcast_to([128, 16, 32])),
                tag=f"a{L}c{c}{which}r2{m}")
            p.emit("dve", lambda v, tok=tok, cosv=cosv: v.tensor_mul(
                tok.rearrange("p (h e) -> p h e", e=DH),
                tok.rearrange("p (h e) -> p h e", e=DH),
                cosv.unsqueeze(1).broadcast_to([128, 16, DH])),
                tag=f"a{L}c{c}{which}r3{m}")
            p.emit("dve", lambda v, tok=tok, tmp=tmp: v.tensor_add(
                tok, tok, tmp), tag=f"a{L}c{c}{which}r4{m}")
        m_rotdone = kb.fence("dve")
    m_proc = kb.fence("dve")

    ptb = kb.pb[7]
    pt_ap = ptb.ap.bitcast(BF16)
    for which in ("q", "k"):
        for h in range(AH):
            src = V["qtok"] if which == "q" else V["ktok"]
            kb.mm_group(ptb, [
                (lambda t, st, sp, m=m, h=h, src=src: t.transpose(
                    pt_ap[0:64, m * 128:(m + 1) * 128],
                    src[:, m * 1024 + h * DH: m * 1024 + (h + 1) * DH],
                    kb.ident[:]))
                for m in range(NT)], waits=[m_proc, kb.m_const],
                tag=f"a{L}c{c}T{which}{h}")
            if which == "q":
                dst = V["qTh"][h][:, :]
            else:
                dst = V["kTh"][h][:, 256:768]
            p.emit("dve", lambda v, dst=dst: v.tensor_copy(dst, pt_ap[0:64, 0:512]),
                   waits=ptb.r() + ([kb.m_ringcopy] if which == "k" else []),
                   inc="dvp", tag=f"a{L}c{c}Te{which}{h}")
            ptb.read(p.mark("dvp"))
    m_T = kb.fence("dve")

    for h in range(AH):
        bo = kb.pb[4 + (h % 2)]
        ow = bo.open_round()
        for m in range(NT):
            bs = kb.pb[m]
            mms = [(lambda t, st, sp, h=h, m=m, bs=bs: t.matmul(
                bs.ap[:, 0:384], V["qTh"][h][:, m * 128:(m + 1) * 128],
                V["kTh"][h][:, m * 128: m * 128 + 384], start=st, stop=sp))]
            if c == 0 and m < 2:
                mms.append(lambda t, st, sp, bs=bs: t.matmul(
                    bs.ap[:, 0:128], kb.ident[:], kb.mask_neg[:],
                    start=st, stop=sp))
                if m == 0:
                    mms.append(lambda t, st, sp, bs=bs: t.matmul(
                        bs.ap[:, 128:256], kb.ident[:], kb.mask_neg[:],
                        start=st, stop=sp))
            else:
                mms.append(lambda t, st, sp, bs=bs: t.matmul(
                    bs.ap[:, 0:128], kb.ident[:], kb.mask_first[:],
                    start=st, stop=sp))
            mms.append(lambda t, st, sp, bs=bs: t.matmul(
                bs.ap[:, 256:384], kb.ident[:], kb.mask_last[:],
                start=st, stop=sp))
            kb.mm_group(bs, mms, waits=[m_T], tag=f"a{L}c{c}S{h}_{m}")
            p.emit("dve", lambda v, bs=bs, m=m: v.reduce_max(
                kb.colsA[:, m:m + 1], bs.ap[:, 0:384], axis=AX.X, negate=True),
                waits=bs.r() + [p.mark("acp")], tag=f"a{L}c{c}mx{h}_{m}")
        m_mx = kb.fence("dve")
        for m in range(NT):
            bs = kb.pb[m]
            p.emit("act", lambda s, bs=bs, m=m: s.activation(
                kb.osq[:, m * 384:(m + 1) * 384], bs.ap[:, 0:384], AF.Exp,
                bias=kb.colsA[:, m:m + 1], accum_out=kb.colsA[:, 4 + m:5 + m]),
                waits=[m_mx], tag=f"a{L}c{c}ex{h}_{m}")
        m_ex = kb.fence("act")
        for m in range(NT):
            kb.pb[m].read(m_ex)
        p.emit("dve", lambda v: v.reciprocal(kb.colsA[:, 8:12],
                                             kb.colsA[:, 4:8]),
               waits=[m_ex], tag=f"a{L}c{c}rz{h}")
        p.emit("dve", lambda v: v.engine_nop(), tag="sp5")
        p.emit("dve", lambda v: v.engine_nop(), inc="dvp", tag=f"a{L}c{c}rzf{h}")
        for m in range(NT):
            p.emit("dve", lambda v, m=m: v.tensor_scalar(
                kb.osq[:, m * 384:(m + 1) * 384],
                kb.osq[:, m * 384:(m + 1) * 384],
                kb.colsA[:, 8 + m:9 + m], None,
                ALU.mult), tag=f"a{L}c{c}an{h}_{m}")
            m_an = kb.fence("dve")
            kb.mm_group(ptb, [
                (lambda t, st, sp, kt=kt, m=m: t.transpose(
                    pt_ap[:, kt * 128:(kt + 1) * 128],
                    kb.osq[:, m * 384 + kt * 128: m * 384 + (kt + 1) * 128],
                    kb.ident[:]))
                for kt in range(3)], waits=[m_an], tag=f"a{L}c{c}TA{h}_{m}")
            p.emit("dve", lambda v: v.tensor_copy(V["atb"][:, 0:384],
                                                  pt_ap[:, 0:384]),
                   waits=ptb.r(), inc="dvp", tag=f"a{L}c{c}TAe{h}_{m}")
            ptb.read(p.mark("dvp"))
            m_atb = kb.fence("dve")
            for kt in range(3):
                gtile = 4 * c + m + kt - 2
                if gtile >= 4 * c:
                    par, mt = pc, m + kt - 2
                else:
                    par, mt = 1 - pc, m + kt + 2
                st = bo.start_flag()
                p.emit("pe", lambda t, h=h, kt=kt, m=m, par=par, mt=mt, st=st,
                       bo=bo: t.matmul(
                    bo.ap[0:64, m * 128:(m + 1) * 128],
                    V["vring"][:, par * 4096 + mt * 1024 + h * DH:
                               par * 4096 + mt * 1024 + (h + 1) * DH],
                    V["atb"][:, kt * 128:(kt + 1) * 128],
                    start=st, stop=(m == NT - 1 and kt == 2)),
                    waits=([m_atb] + (ow if (m == 0 and kt == 0) else [])),
                    inc="pew" if (m == NT - 1 and kt == 2) else None,
                    tag=f"a{L}c{c}av{h}_{m}_{kt}")
        bo.close_round(p.mark("pew"))
        jj = h // 2
        p.emit("dve", lambda v, h=h, jj=jj, bo=bo: v.tensor_copy(
            V["aoT"][:, jj * CH:(jj + 1) * CH][(h % 2) * 64:(h % 2) * 64 + 64, :],
            bo.ap[0:64, :]), waits=bo.r(), inc="dvp", tag=f"a{L}c{c}oe{h}")
        bo.read(p.mark("dvp"))
    m_ao = kb.fence("dve")

    m_pe_now = p.mark("pew")
    for h in range(AH):
        p.emit("dve", lambda v, h=h: v.tensor_copy(
            V["kTh"][h][:, 0:256], V["kTh"][h][:, 512:768]),
            waits=[m_T, m_pe_now], inc="dvp" if h == AH - 1 else None,
            tag=f"a{L}c{c}rc{h}")
    kb.m_ringcopy = p.mark("dvp")

    dmw = []
    for dm in range(8):
        dmw += kb.pb[dm].open_round()
    for grp in range(2):
        par = kb.ws_par = (kb.ws_par + 1) % 2
        mld = kb.dma("pool", f"ldw{par}", lambda g, grp=grp, par=par: g.dma_start(
            out=ws[par][:, 0:4096].rearrange("p (k n) -> p k n", n=1024),
            in_=kb.wo_ffn_f[L, grp * 512:(grp + 1) * 512, :].rearrange(
                "(k p) n -> p k n", p=128)),
            waits=[kb.ws_free[par]], tag=f"a{L}c{c}wo{grp}")
        for dm in range(8):
            bo = kb.pb[dm]
            for kk in range(4):
                st = bo.start_flag()
                w = []
                if dm == 0 and kk == 0:
                    w = [mld, m_ao] + (dmw if grp == 0 else [])
                p.emit("pe", lambda t, dm=dm, kk=kk, grp=grp, par=par, st=st,
                       bo=bo: t.matmul(
                    bo.ap[:, :],
                    ws[par][:, kk * 1024 + dm * 128: kk * 1024 + (dm + 1) * 128],
                    V["aoT"][:, (grp * 4 + kk) * CH:(grp * 4 + kk + 1) * CH],
                    start=st, stop=(grp == 1 and kk == 3)),
                    waits=w, inc="pew" if (dm == 7 and kk == 3) else None,
                    tag=f"a{L}c{c}WO{grp}_{dm}_{kk}")
        kb.ws_free[par] = p.mark("pew")
    m_wo = p.mark("pew")
    for dm in range(8):
        kb.pb[dm].close_round(m_wo)
    for dm in range(8):
        bo = kb.pb[dm]
        p.emit("dve", lambda v, dm=dm, bo=bo: v.tensor_add(
            ych[:, dm * CH:(dm + 1) * CH], ych[:, dm * CH:(dm + 1) * CH],
            bo.ap[:, :]), waits=bo.r() + [m_y], inc="dvp", tag=f"a{L}c{c}res{dm}")
        bo.read(p.mark("dvp"))
    m_res = p.mark("dvp")
    kb.dma("sp", "stsp", lambda s: s.dma_start(
        out=y_out[:, t0:t0 + CH].rearrange("(j p) t -> p j t", p=128),
        in_=ych[:, :].rearrange("p (j t) -> p j t", t=CH)),
        waits=[m_res], tag=f"a{L}c{c}yst")
    kb.ych_free[pc] = m_res
    kb.ysh_free[pc] = p.mark("pew")



def ffn_layer(kb, L):
    p = kb.p
    y_in, y_out = kb.ybuf[1], kb.ybuf[0]
    actT = kb.arena_bf(0, 11264)
    xbfA = kb.arena_bf(22, 2048)
    sqbA = kb.arena_bf(26, 4096)
    gelu2 = [kb.arena_bf(34, 512), kb.arena_bf(35, 512)]
    kb._sqb_fn = lambda j: sqbA[:, j * CH:(j + 1) * CH]
    gate_w = [p.mark("pew"), p.mark("dvp"), p.mark("acp")]
    for sb, hb in ((kb.lng_sb, kb.ln_g_f), (kb.lnb_sb, kb.ln_b_f),
                   (kb.b2_sb, kb.b2_f), (kb.b1a_sb, kb.b1a_f),
                   (kb.b1g_sb, kb.b1g_f)):
        kb.dma("pool", "lpar", lambda g, sb=sb, hb=hb: g.dma_start(
            out=sb[:, :], in_=hb[L, :, :]), waits=gate_w, tag=f"f{L}par")
    m_par = p.mark("lpar")
    for c in range(NCHUNK):
        ffn_chunk(kb, L, c, y_in, y_out, actT, xbfA, gelu2, m_par)


def ffn_chunk(kb, L, c, y_in, y_out, actT, xbfA, gelu2, m_par):
    p = kb.p
    pc = c % 2
    t0 = c * CH
    ych = kb.ych[pc]
    ysh = kb.ysh[pc]
    ws = kb.ws

    m_y = kb.dma("sp", f"ldy{pc}", lambda s: s.dma_start(
        out=ych[:, :].rearrange("p (j t) -> p j t", t=CH),
        in_=y_in[:, t0:t0 + CH].rearrange("(j p) t -> p j t", p=128)),
        waits=[kb.ych_free[pc]], tag=f"f{L}c{c}y")
    st_now = kb.store_marks()
    if c == 0:
        for j in range(4):
            p.emit("dve", lambda v, j=j, ysh=ysh: v.memset(
                ysh[:, j * CH:j * CH + 1], 0.0),
                waits=[kb.ysh_free[pc]], inc="dvp", tag=f"f{L}c0z{j}")
        kb.dma("pool", f"ldsh{pc}", lambda g: g.dma_start(
            out=ysh[:, :].rearrange("p (j t) -> p j t", t=CH)[:, :, 1:CH],
            in_=y_in[512:1024, 0:CH - 1].rearrange("(j p) t -> p j t", p=128)),
            waits=[kb.ysh_free[pc]] + st_now, tag=f"f{L}c{c}ys")
        m_ysh = [p.mark(f"ldsh{pc}"), p.mark("dvp")]
    else:
        m1 = kb.dma("pool", f"ldsh{pc}", lambda g: g.dma_start(
            out=ysh[:, :].rearrange("p (j t) -> p j t", t=CH),
            in_=y_in[512:1024, t0 - 1:t0 + CH - 1].rearrange(
                "(j p) t -> p j t", p=128)),
            waits=[kb.ysh_free[pc]] + st_now, tag=f"f{L}c{c}ys")
        m_ysh = [m1]

    for j in range(4):
        p.emit("dve", lambda v, j=j: v.tensor_copy(
            xbfA[:, j * CH:(j + 1) * CH], ych[:, j * CH:(j + 1) * CH]),
            waits=[m_y, p.mark("pew")], tag=f"f{L}c{c}lc{j}")
    m_low = kb.fence("dve")

    def ys_bf(j):
        if j < 4:
            return xbfA[:, j * CH:(j + 1) * CH]
        return ysh[:, (j - 4) * CH:(j - 3) * CH]

    def ys_any(j):
        if j < 4:
            return ych[:, j * CH:(j + 1) * CH]
        return ysh[:, (j - 4) * CH:(j - 3) * CH]

    ba, bb = kb.pb[0], kb.pb[1]
    m_rep = kb.ln_stats(ys_bf, ys_any, 8, ba, bb,
                        [m_low, p.mark("dvp"), p.mark("pew")] + m_ysh,
                        f"f{L}c{c}LN")
    for j in range(8):
        zt = kb.ztmp[j % 2]
        p.emit("dve", lambda v, j=j, zt=zt: v.tensor_tensor(
            zt[:], ys_any(j), ba.ap[:, :], ALU.mult),
            waits=[m_rep, m_par], tag=f"f{L}c{c}xa{j}")
        p.emit("dve", lambda v, zt=zt: v.tensor_tensor(
            zt[:], zt[:], bb.ap[:, :], ALU.add), tag=f"f{L}c{c}xb{j}")
        p.emit("dve", lambda v, j=j, zt=zt: v.tensor_scalar(
            kb.zbf[:, j * CH:(j + 1) * CH], zt[:],
            kb.lng_sb[:, j:j + 1], kb.lnb_sb[:, j:j + 1], ALU.mult, ALU.add),
            tag=f"f{L}c{c}xc{j}")
    m_xn = kb.fence("dve")
    ba.read(m_xn); bb.read(m_xn)
    if kb.debug and L == 0 and c == 7:
        kb.dma("pool", "ldc", lambda g: g.dma_start(
            out=kb.dbg[0][0:1, 0:512], in_=kb.st1[:, :]), waits=[m_xn],
            tag="fdbg0")
        kb.dma("pool", "ldc", lambda g: g.dma_start(
            out=kb.dbg[0][:, 2048:4096], in_=xbfA[:, :]), waits=[m_low],
            tag="fdbgX")
        kb.dma("pool", "ldc", lambda g, ych=ych: g.dma_start(
            out=kb.dbg[1][:, 512:512 + 2048], in_=ych[:, 0:2048]),
            waits=[m_low], tag="fdbgY")
        kb.dma("pool", "ldc", lambda g: g.dma_start(
            out=kb.dbg[1][0:1, 0:512], in_=kb.st2[:, :]), waits=[m_xn],
            tag="fdbg1")
        kb.dma("pool", "ldc", lambda g: g.dma_start(
            out=kb.dbg[2][:, 0:2048], in_=xbfA[:, :]), waits=[m_xn],
            tag="fdbg2")
        kb.dma("pool", "ldc", lambda g: g.dma_start(
            out=kb.dbg[3][:, 0:2048], in_=ysh[:, :]), waits=[m_xn],
            tag="fdbg3")
        m_dbg = p.mark("ldc")
        p.emit("dve", lambda v: v.engine_nop(), waits=[m_dbg], tag="fdbgs")
        p.emit("act", lambda s: s.activation(kb.acsp[:], kb.acsp[:],
                                             AF.Identity), waits=[m_dbg],
               tag="fdbgs2")

    gl_marks = [None, None]
    for grp in range(11):
        par = kb.ws_par = (kb.ws_par + 1) % 2
        mld = kb.dma("pool", f"ldw{par}", lambda g, grp=grp, par=par: g.dma_start(
            out=ws[par][:, 0:4096].rearrange("p (k n) -> p k n", n=CH),
            in_=kb.w1ag_f[L, :, grp * CH:(grp + 1) * CH].rearrange(
                "(k p) n -> p k n", p=128)),
            waits=[kb.ws_free[par]], tag=f"f{L}c{c}w1_{grp}")
        for jj in range(2):
            j = grp * 2 + jj
            bha, bhg = kb.pb[2], kb.pb[3]
            kb.mm_group(bha, [
                (lambda t, st, sp, k=k, jj=jj, par=par: t.matmul(
                    bha.ap[:, :],
                    ws[par][:, k * CH + jj * 256: k * CH + jj * 256 + 128],
                    kb.zbf[:, k * CH:(k + 1) * CH], start=st, stop=sp))
                for k in range(8)], waits=[m_xn, mld], tag=f"f{L}c{c}ha{j}")
            kb.mm_group(bhg, [
                (lambda t, st, sp, k=k, jj=jj, par=par: t.matmul(
                    bhg.ap[:, :],
                    ws[par][:, k * CH + jj * 256 + 128: k * CH + jj * 256 + 256],
                    kb.zbf[:, k * CH:(k + 1) * CH], start=st, stop=sp))
                for k in range(8)], waits=[mld], tag=f"f{L}c{c}hg{j}")
            gsl = gelu2[j % 2]
            p.emit("act", lambda s, j=j, bhg=bhg, gsl=gsl: s.activation(
                gsl[:, :], bhg.ap[:, :], AF.Gelu,
                bias=kb.b1g_sb[:, j:j + 1]),
                waits=bhg.r() + [m_par, gl_marks[j % 2]], inc="acp",
                tag=f"f{L}c{c}gel{j}")
            m_gel = p.mark("acp")
            bhg.read(m_gel)
            p.emit("dve", lambda v, j=j, bha=bha, gsl=gsl: v.scalar_tensor_tensor(
                actT[:, j * CH:(j + 1) * CH], bha.ap[:, :],
                kb.b1a_sb[:, j:j + 1], gsl[:, :], ALU.add, ALU.mult),
                waits=bha.r() + [m_gel], inc="dvp", tag=f"f{L}c{c}act{j}")
            gl_marks[j % 2] = p.mark("dvp")
            bha.read(p.mark("dvp"))
        kb.ws_free[par] = p.mark("dvp")
    m_act = p.mark("dvp")

    dmw = []
    for dm in range(8):
        dmw += kb.pb[dm].open_round()
    NGRP = 6
    for grp in range(NGRP):
        nk = 4 if grp < 5 else 2
        par = kb.ws_par = (kb.ws_par + 1) % 2
        r0 = grp * 512
        nrow = nk * 128
        mld = kb.dma("pool", f"ldw{par}", lambda g, r0=r0, nrow=nrow, par=par: g.dma_start(
            out=ws[par][:, 0:nrow * 8].rearrange("p (k n) -> p k n", n=1024),
            in_=kb.w2p_f[L, r0:r0 + nrow, :].rearrange(
                "(k p) n -> p k n", p=128)),
            waits=[kb.ws_free[par]], tag=f"f{L}c{c}w2_{grp}")
        for dm in range(8):
            bo = kb.pb[dm]
            for kk in range(nk):
                kglob = grp * 4 + kk
                st = bo.start_flag()
                w = []
                if dm == 0 and kk == 0:
                    w = [mld, m_act] + (dmw if grp == 0 else [])
                p.emit("pe", lambda t, dm=dm, kk=kk, kglob=kglob, par=par,
                       st=st, bo=bo: t.matmul(
                    bo.ap[:, :],
                    ws[par][:, kk * 1024 + dm * 128: kk * 1024 + (dm + 1) * 128],
                    actT[:, kglob * CH:(kglob + 1) * CH],
                    start=st, stop=(kglob == NJ - 1)),
                    waits=w,
                    inc="pew" if (dm == 7 and kk == nk - 1) else None,
                    tag=f"f{L}c{c}W2_{grp}_{dm}_{kk}")
        kb.ws_free[par] = p.mark("pew")
    m_w2 = p.mark("pew")
    for dm in range(8):
        kb.pb[dm].close_round(m_w2)
    for dm in range(8):
        bo = kb.pb[dm]
        p.emit("dve", lambda v, dm=dm, bo=bo: v.scalar_tensor_tensor(
            ych[:, dm * CH:(dm + 1) * CH], bo.ap[:, :],
            kb.b2_sb[:, dm:dm + 1], ych[:, dm * CH:(dm + 1) * CH],
            ALU.add, ALU.add),
            waits=bo.r() + [m_y, m_par], inc="dvp", tag=f"f{L}c{c}res{dm}")
        bo.read(p.mark("dvp"))
    m_res = kb.fence("dve")
    kb.dma("sp", "stsp", lambda s: s.dma_start(
        out=y_out[:, t0:t0 + CH].rearrange("(j p) t -> p j t", p=128),
        in_=ych[:, :].rearrange("p (j t) -> p j t", t=CH)),
        waits=[m_res], tag=f"f{L}c{c}yst")
    kb.ych_free[pc] = m_res
    kb.ysh_free[pc] = p.mark("pew")


def epilogue(kb):
    p = kb.p
    x1b = kb.arena_f32(0, 4096)
    xbfA = kb.arena_bf(16, 4096)
    sqbA = kb.arena_bf(24, 4096)
    kb._sqb_fn = lambda j: sqbA[:, j * CH:(j + 1) * CH]
    for c in range(NCHUNK):
        pc = c % 2
        t0 = c * CH
        ych = kb.ych[pc]
        st_now = kb.store_marks() if c == 0 else []
        m_y = kb.dma("sp", f"ldy{pc}", lambda s, t0=t0, ych=ych: s.dma_start(
            out=ych[:, :].rearrange("p (j t) -> p j t", t=CH),
            in_=kb.ybuf[0][:, t0:t0 + CH].rearrange("(j p) t -> p j t", p=128)),
            waits=[kb.ych_free[pc]] + st_now, tag=f"e{c}y")
        m_x1 = kb.dma("sp", "ldx1", lambda s, t0=t0: s.dma_start(
            out=x1b[:, :].rearrange("p (j t) -> p j t", t=CH),
            in_=kb.xbuf[:, t0:t0 + CH].rearrange("(j p) t -> p j t", p=128)),
            waits=([kb.x1b_free] if getattr(kb, "x1b_free", None) else
                   [p.mark("pew")]) + st_now, tag=f"e{c}x1")
        ych_j = lambda j, ych=ych: ych[:, j * CH:(j + 1) * CH]
        xbf_j = lambda j: xbfA[:, j * CH:(j + 1) * CH]
        ba, bb = kb.pb[0], kb.pb[1]
        m_rep = kb.ln_stats(xbf_j, ych_j, 8, ba, bb,
                            [m_y, p.mark("dvp"), p.mark("pew")], f"e{c}LN",
                            pre_cast_to=xbf_j)
        st_marks = [None, None]
        for j in range(8):
            zt = kb.ztmp[j % 2]
            p.emit("dve", lambda v, j=j, zt=zt, ych=ych: v.tensor_tensor(
                zt[:], ych[:, j * CH:(j + 1) * CH], ba.ap[:, :], ALU.mult),
                waits=[m_rep, m_x1, st_marks[j % 2]], tag=f"e{c}oa{j}")
            p.emit("dve", lambda v, zt=zt: v.tensor_tensor(
                zt[:], zt[:], bb.ap[:, :], ALU.add), tag=f"e{c}ob{j}")
            p.emit("dve", lambda v, j=j, zt=zt: v.tensor_tensor(
                zt[:], zt[:], x1b[:, j * CH:(j + 1) * CH], ALU.add),
                tag=f"e{c}oc{j}")
            m_j = kb.fence("dve")
            st_marks[j % 2] = kb.dma("sp", f"stz{j % 2}",
                lambda s, j=j, zt=zt, t0=t0: s.dma_start(
                    out=kb.outp[j * 128:(j + 1) * 128, t0:t0 + CH], in_=zt[:]),
                waits=[m_j], tag=f"e{c}st{j}")
        m_done = kb.fence("dve")
        ba.read(m_done); bb.read(m_done)
        kb.ych_free[pc] = m_done
        kb.x1b_free = m_done


def build_all(kb):
    kb.load_constants()
    gla_phase(kb)
    for L in range(kb.n_layers):
        attn_layer(kb, L)
        ffn_layer(kb, L)
    epilogue(kb)
    p = kb.p
    p.emit("sp", lambda s: s.nop(), waits=kb.store_marks(), tag="final")


def finalize(kb):
    nc, p = kb.nc, kb.p
    for s in p.sems.values():
        s.h = nc.alloc_semaphore(name=s.name)
    with nc.Block() as block:
        @block.tensor
        def _(te):
            p.replay("pe", te)

        @block.vector
        def _(v):
            p.replay("dve", v)

        @block.scalar
        def _(sc):
            p.replay("act", sc)

        @block.gpsimd
        def _(g):
            p.replay("pool", g)

        @block.sync
        def _(s):
            p.replay("sp", s)
    return nc


def build_program(n_layers=6):
    kb = KB(n_layers)
    build_all(kb)
    finalize(kb)
    return kb


def make_in_maps(inputs, per_core_x=None, W=None, C=None):
    if per_core_x is None:
        per_core_x, W, C = host_prep(inputs)
    base = {
        "wq": W["wq"], "wk": W["wk"], "wv": W["wv"], "wg": W["wg"],
        "gkw1": W["gkw1"], "gkw2": W["gkw2"], "gkb": W["gkb"],
        "wo_gla": W["wo_gla"], "rmsw_cols": W["rmsw_cols"],
        "wqkv": W["wqkv"], "qs": W["qs_row"], "ks": W["ks_row"],
        "wo_ffn": W["wo_ffn"], "ln_g": W["ln_g"], "ln_b": W["ln_b"],
        "w1ag": W["w1ag"], "b1a_cols": W["b1a_cols"], "b1g_cols": W["b1g_cols"],
        "w2p": W["w2p"], "b2": W["b2"],
        "ident": C["ident"], "lt16": C["lt16"], "tri01": C["tri01"],
        "mask_first": C["mask_first"], "mask_last": C["mask_last"],
        "mask_neg": C["mask_neg"],
        "ones_col": C["ones_col"], "ones_row": C["ones_row"],
        "fconsts": C["fconsts"], "fconsts_col": C["fconsts_col"],
        "rot_cs": C["rot_cs"], "rot_ssx": C["rot_ssx"],
        "rot_ci": C["rot_ci"], "rot_six": C["rot_six"],
    }
    in_maps = []
    for b in range(NCORES):
        m = dict(base)
        m["x0"] = per_core_x[b]
        in_maps.append(m)
    return in_maps


def kernel(**inputs):
    kb = build_program(6)
    in_maps = make_in_maps(inputs)
    res = run_bass_kernel_spmd(kb.nc, in_maps, core_ids=list(range(NCORES)))
    return np.stack([np.asarray(r["outp"], np.float32).T for r in res.results])
